# version 18 (final)
# speedup vs baseline: 1.2107x; 1.0157x over previous
import contextlib
import numpy as np
import ml_dtypes
import concourse.bass as bass
import concourse.mybir as mybir
from concourse.aot_env import allow_ici_all_to_all
from concourse.bass_utils import run_bass_kernel_spmd

F32 = mybir.dt.float32
BF16 = mybir.dt.bfloat16
I32 = mybir.dt.int32
AF = mybir.ActivationFunctionType
ALU = mybir.AluOpType
AX = mybir.AxisListType

D = 2048
L = 4096
DSL = 512
NDB = 4
TCH = 512
NCHUNK = L // TCH
NST = 16
HID = 8192
TFC = 128
NFC = 8
NSL = 16
EPS = 1e-5


def build_nc(debug=()):
    nc = bass.Bass("TRN2", target_bir_lowering=False)

    uT = nc.dram_tensor("uT", [D, L], F32, kind="ExternalInput").ap()
    uS = nc.dram_tensor("uS", [DSL, L], F32, kind="ExternalInput").ap()
    w3 = nc.dram_tensor("w3", [D, 3 * DSL], F32, kind="ExternalInput").ap()
    biases = nc.dram_tensor("biases", [128, 12], F32, kind="ExternalInput").ap()
    araw = nc.dram_tensor("araw", [128, NDB * NST], F32, kind="ExternalInput").ap()
    bbase = nc.dram_tensor("bbase", [128, NDB * NST], F32, kind="ExternalInput").ap()
    cbase = nc.dram_tensor("cbase", [128, NDB * NST], F32, kind="ExternalInput").ap()
    dvec = nc.dram_tensor("dvec", [128, NDB], F32, kind="ExternalInput").ap()
    nwrep = nc.dram_tensor("nwrep", [128, D], F32, kind="ExternalInput").ap()
    idb = nc.dram_tensor("idb", [128, 128], BF16, kind="ExternalInput").ap()
    idf = nc.dram_tensor("idf", [128, 128], F32, kind="ExternalInput").ap()
    wgvp = nc.dram_tensor("wgvp", [D, 2 * HID], F32, kind="ExternalInput").ap()
    woutT = nc.dram_tensor("woutT", [HID, D], F32, kind="ExternalInput").ap()
    wmg = nc.dram_tensor("wmg", [D, 2048], F32, kind="ExternalInput").ap()
    wmo = nc.dram_tensor("wmo", [HID, 256], F32, kind="ExternalInput").ap()
    out = nc.dram_tensor("out", [NFC * TFC, D], F32, kind="ExternalOutput").ap()

    a2a_in = nc.dram_tensor("a2a_in", [L, DSL], F32, kind="Internal")
    a2a_out = nc.dram_tensor("a2a_out", [L, DSL], F32, kind="Internal")
    mean_in = nc.dram_tensor("mean_in", [1, 2], F32, kind="Internal")
    mean_out = nc.dram_tensor("mean_out", [1, 2], F32, kind="Internal")
    h2sp = nc.dram_tensor("h2sp", [NFC * TFC, HID], F32, kind="Internal")
    qh2sp = nc.dram_tensor("qh2sp", [128, 64 * NFC * TFC], BF16, kind="Internal")
    uTb_d = nc.dram_tensor("uTb_d", [D, L], BF16, kind="Internal")

    dbg = {}
    dbg_shapes = {"z": ([L, DSL], F32), "cst": ([128, 208], F32), "f6": ([128, 3 * D + 16 + 8], F32),
                  "qh": ([128, 16 * NFC * TFC], BF16), "h2": ([NFC * TFC, HID], F32),
                  "ob": ([128, NFC * D], F32)}
    for name in debug:
        shp, dt_ = dbg_shapes[name]
        dbg[name] = nc.dram_tensor("dbg_" + name, shp, dt_, kind="ExternalOutput").ap()

    ctx = contextlib.ExitStack()
    sem = {}
    cnt = {}
    for s in ("ve", "se", "pe", "gp", "cc"):
        sem[s] = ctx.enter_context(nc.semaphore("s_" + s))
        cnt[s] = 0

    def inc(s, instr, k=1):
        cnt[s] += k
        instr.then_inc(sem[s], k)
        return instr

    def wait(eng, s, val=None):
        eng.wait_ge(sem[s], cnt[s] if val is None else val)

    class DS:
        def __init__(self, name):
            self.h = ctx.enter_context(nc.semaphore(name))
            self.n = 0
        def inc(self, instr):
            self.n += 16
            instr.then_inc(self.h, 16)
            return instr
        def wait(self, eng, val=None):
            eng.wait_ge(self.h, self.n if val is None else val)

    ds_p0 = DS("d_p0")
    ds_stg = [DS(f"d_stg{i}") for i in range(6)]
    ds_us = [DS(f"d_us{i}") for i in range(2)]
    ds_zt = [DS(f"d_zt{i}") for i in range(2)]
    ds_wm = [DS(f"d_wm{i}") for i in range(4)]
    ds_nw = DS("d_nw")
    ds_mean = DS("d_mean")
    ds_fz = [DS(f"d_fz{i}") for i in range(2)]
    ds_wg = [DS(f"d_wg{i}") for i in range(2)]
    ds_h2 = [DS(f"d_h2{i}") for i in range(4)]
    ds_h2r = [DS(f"d_h2r{i}") for i in range(2)]
    ds_q2 = DS("d_q2")
    ds_w2 = [DS(f"d_w2{i}") for i in range(4)]
    ds_qt = [DS(f"d_qt{i}") for i in range(4)]
    ds_out = [DS(f"d_out{i}") for i in range(2)]
    ds_gz = DS("d_gz")
    ds_dbg = DS("d_dbg")
    ds_cvin = [DS(f"d_cvin{i}") for i in range(3)]
    ds_cvout = [DS(f"d_cvout{i}") for i in range(3)]
    ds_ut = [DS(f"d_ut{i}") for i in range(2)]

    V, S, T, G, Y = nc.vector, nc.scalar, nc.tensor, nc.gpsimd, nc.sync

    A = ctx.enter_context
    c_idb = A(nc.sbuf_tensor("c_idb", [128, 128], BF16))
    c_idf = A(nc.sbuf_tensor("c_idf", [128, 128], F32))
    c_bias = A(nc.sbuf_tensor("c_bias", [128, 12], F32))
    c_acol = A(nc.sbuf_tensor("c_acol", [128, NDB * NST], F32))
    c_bccol = A(nc.sbuf_tensor("c_bccol", [128, NDB * NST], F32))
    c_d1col = A(nc.sbuf_tensor("c_d1col", [128, NDB], F32))
    c_carry = A(nc.sbuf_tensor("c_carry", [128, NDB * NST], F32))
    c_rows = A(nc.sbuf_tensor("c_rows", [128, 2 * NFC], F32))
    c_sm = A(nc.sbuf_tensor("c_sm", [128, 8], F32))
    c_mrow = A(nc.sbuf_tensor("c_mrow", [1, 2], F32))
    c_ones = A(nc.sbuf_tensor("c_ones", [128, 1], F32))
    c_onesr = A(nc.sbuf_tensor("c_onesr", [1, 128], F32))
    c_rmax2 = A(nc.sbuf_tensor("c_rmax2", [128, NFC * NSL], F32))

    blk = ctx.enter_context(nc.Block())

    ds_p0.inc(Y.dma_start(out=c_idb[:], in_=idb))
    ds_p0.inc(Y.dma_start(out=c_idf[:], in_=idf))
    ds_p0.inc(Y.dma_start(out=c_bias[:], in_=biases))
    ds_p0.inc(Y.dma_start(out=c_acol[:], in_=araw))
    ds_p0.inc(Y.dma_start(out=c_bccol[:], in_=bbase))
    ds_p0.inc(Y.dma_start(out=c_rmax2[:, 0:NDB * NST], in_=cbase))
    ds_p0.inc(Y.dma_start(out=c_d1col[:], in_=dvec))

    ds_p0.wait(S)
    S.activation(out=c_acol[:], in_=c_acol[:], func=AF.Exp, bias=0.0, scale=1.0)
    S.drain()
    S.activation(out=c_acol[:], in_=c_acol[:], func=AF.Ln, bias=1.0, scale=1.0)
    S.drain()
    inc("se", S.activation(out=c_acol[:], in_=c_acol[:], func=AF.Copy, bias=0.0, scale=-1.0))
    S.drain()

    ds_p0.wait(V)
    V.tensor_tensor(out=c_bccol[:], in0=c_bccol[:], in1=c_rmax2[:, 0:NDB * NST], op=ALU.mult)
    V.tensor_scalar(out=c_d1col[:], in0=c_d1col[:], scalar1=1.0, scalar2=None, op0=ALU.add)
    V.memset(c_carry[:], 0.0)
    V.memset(c_sm[:, 4:5], EPS)
    V.memset(c_onesr[:], 1.0)
    inc("ve", V.memset(c_ones[:], 1.0))
    V.drain()
    ve_consts = cnt["ve"]
    if "cst" in dbg:
        wait(G, "ve", ve_consts)
        ds_dbg.inc(G.dma_start(out=dbg["cst"][:, 0:4], in_=c_d1col[:]))
        ds_dbg.inc(G.dma_start(out=dbg["cst"][:, 8:72], in_=c_acol[:]))

    wm_state = {"pe": [0, 0], "pe_last": 0}

    def emit_wmean_unit(k, m_wmt, p_wm, ve_consts):
        sl = k % 2
        if k >= 2:
            wait(G, "pe", wm_state["pe"][sl])
        w = 512 if k < 64 else 256
        if k < 64:
            srcw = wmg[(k // 4) * 128:(k // 4 + 1) * 128, (k % 4) * 512:(k % 4 + 1) * 512]
        else:
            srcw = wmo[(k - 64) * 128:(k - 63) * 128, :]
        ds_wm[sl].inc(G.dma_start(out=m_wmt[:, sl * 512: sl * 512 + w], in_=srcw))
        ds_wm[sl].wait(S)
        if k >= 2:
            wait(S, "pe", wm_state["pe"][sl])
        inc("se", S.activation(out=m_wmt[:, sl * 512: sl * 512 + w],
                               in_=m_wmt[:, sl * 512: sl * 512 + w],
                               func=AF.Abs, bias=0.0, scale=1.0))
        wait(T, "se", cnt["se"])
        if k == 0:
            wait(T, "ve", ve_consts)
        if k < 64:
            mm = T.matmul(p_wm[0:1, 0:512], c_ones[:, 0:1],
                          m_wmt[:, sl * 512: sl * 512 + 512],
                          start=(k == 0), stop=(k == 63))
        else:
            mm = T.matmul(p_wm[0:1, 512:768], c_ones[:, 0:1],
                          m_wmt[:, sl * 512: sl * 512 + 256],
                          start=(k == 64), stop=(k == 127))
        inc("pe", mm)
        wm_state["pe"][sl] = cnt["pe"]
        wm_state["pe_last"] = cnt["pe"]

    with contextlib.ExitStack() as _st:
        m_stg = _st.enter_context(nc.sbuf_tensor("m_stg", [128, 3 * 512], F32))
        m_pb = _st.enter_context(nc.sbuf_tensor("m_pb", [128, 3 * 512], BF16))
        m_uTb = _st.enter_context(nc.sbuf_tensor("m_uTb", [128, 16 * TCH], BF16))
        m_us = _st.enter_context(nc.sbuf_tensor("m_us", [128, 2 * NDB * TCH], F32))
        m_w3b = _st.enter_context(nc.sbuf_tensor("m_w3b", [128, 16 * 1536], BF16))
        m_dt = _st.enter_context(nc.sbuf_tensor("m_dt", [128, 2 * NDB * TCH], F32))
        m_bt = _st.enter_context(nc.sbuf_tensor("m_bt", [128, 2 * NDB * TCH], F32))
        m_ct = _st.enter_context(nc.sbuf_tensor("m_ct", [128, 2 * NDB * TCH], F32))
        m_ab = _st.enter_context(nc.sbuf_tensor("m_ab", [128, 2 * 4 * TCH], F32))
        m_xs = _st.enter_context(nc.sbuf_tensor("m_xs", [128, 2 * 4 * TCH], F32))
        m_yv = _st.enter_context(nc.sbuf_tensor("m_yv", [128, 2 * TCH], F32))
        m_yg = _st.enter_context(nc.sbuf_tensor("m_yg", [128, 2 * TCH], F32))
        m_zT = _st.enter_context(nc.sbuf_tensor("m_zT", [128, 2 * DSL], F32))
        m_wmt = _st.enter_context(nc.sbuf_tensor("m_wmt", [128, 2 * 512], F32))
        p_proj = _st.enter_context(nc.psum_tensor("p_proj", [128, 4 * TCH], F32))
        p_tr = _st.enter_context(nc.psum_tensor("p_tr", [128, DSL], F32))
        p_wm = _st.enter_context(nc.psum_tensor("p_wm", [1, 768], F32))

        cv_state = {"n": 0}

        def emit_convert(src_ap, dst_d, rows, cols, ncols):
            i = cv_state["n"]
            sl = i % 3
            if i >= 3:
                ds_cvout[sl].wait(Y)
            ds_cvin[sl].inc(Y.dma_start(out=m_stg[:, sl * 512: sl * 512 + ncols],
                                        in_=src_ap[rows * 128:(rows + 1) * 128, cols:cols + ncols]))
            ds_cvin[sl].wait(S)
            if i >= 3:
                ds_cvout[sl].wait(S)
            inc("se", S.activation(out=m_pb[:, sl * 512: sl * 512 + ncols],
                                   in_=m_stg[:, sl * 512: sl * 512 + ncols],
                                   func=AF.Copy, bias=0.0, scale=1.0))
            wait(S, "se")
            ds_cvout[sl].inc(S.dma_start(
                out=dst_d.ap()[rows * 128:(rows + 1) * 128, cols:cols + ncols],
                in_=m_pb[:, sl * 512: sl * 512 + ncols]))
            cv_state["n"] += 1

        w3c_hist = [0, 0, 0]
        for u in range(48):
            i, h = u // 3, (u % 3) * 512
            sl = u % 3
            if u >= 3:
                wait(Y, "se", w3c_hist[sl])
            ds_cvin[sl].inc(Y.dma_start(out=m_stg[:, sl * 512: sl * 512 + 512],
                                        in_=w3[i * 128:(i + 1) * 128, h:h + 512]))
            ds_cvin[sl].wait(S)
            inc("se", S.activation(out=m_w3b[:, i * 1536 + h: i * 1536 + h + 512],
                                   in_=m_stg[:, sl * 512: sl * 512 + 512],
                                   func=AF.Copy, bias=0.0, scale=1.0))
            w3c_hist[sl] = cnt["se"]
        w3_cast_se = cnt["se"]
        wait(Y, "se", w3_cast_se)
        for c in range(2):
            for i in range(16):
                emit_convert(uT, uTb_d, i, c * 512, 512)

        pe_proj_done = 0
        se_carry = {}
        ve_z_done = 0
        ve_z_hist = [0, 0]
        gp_grp_cons = 0
        ve_scans_hist = [0, 0]
        gp_yacc_hist = [0, 0]
        pe_evac_prev = [0, 0, 0, 0]
        uT_cast_hist = []
        pe_tr_done = 0
        pe_tr_hist = [0, 0]

        fe_state = {}

        def emit_frontend(ch):
            nonlocal pe_proj_done
            t0 = ch * TCH
            utsl = ch % 2
            if ch > 0:
                wait(Y, "pe", pe_proj_done)
            ds_cvout[0].wait(Y)
            ds_cvout[1].wait(Y)
            for i in range(16):
                ds_ut[utsl].inc(Y.dma_start(out=m_uTb[:, i * TCH:(i + 1) * TCH],
                                            in_=uTb_d.ap()[i * 128:(i + 1) * 128, t0:t0 + TCH]))
            usp = ch % 2
            if ch >= 2:
                wait(Y, "ve", ve_z_hist[usp])
            for j in range(NDB):
                ds_us[usp].inc(Y.dma_start(
                    out=m_us[:, (usp * NDB + j) * TCH:(usp * NDB + j + 1) * TCH],
                    in_=uS[j * 128:(j + 1) * 128, t0:t0 + TCH]))

        PROJ_ORDER = [(1, j) for j in range(4)] + [(0, j) for j in range(4)] + [(2, j) for j in range(4)]

        def emit_proj_mm(ch, gidx, evac_se):
            nonlocal pe_proj_done
            pj, j = PROJ_ORDER[gidx]
            par = gidx % 4
            utsl = ch % 2
            ds_ut[utsl].wait(T)
            wait(T, "se", w3_cast_se)
            if gidx >= 4:
                wait(T, "se", evac_se[gidx - 4])
            elif ch > 0:
                wait(T, "se", pe_evac_prev[par])
            for i in range(16):
                mm = T.matmul(p_proj[:, par * TCH:(par + 1) * TCH],
                              m_w3b[:, i * 1536 + pj * 512 + j * 128:
                                    i * 1536 + pj * 512 + (j + 1) * 128],
                              m_uTb[:, i * TCH:(i + 1) * TCH],
                              start=(i == 0), stop=(i == 15))
                if i == 15:
                    inc("pe", mm)
            if gidx == 11:
                pe_proj_done = cnt["pe"]
            return cnt["pe"]

        def emit_proj_evac(ch, gidx, pe_val, evac_se):
            nonlocal pe_evac_prev
            pj, j = PROJ_ORDER[gidx]
            par = gidx % 4
            dpar = ch % 2
            dof = dpar * NDB * TCH
            wait(S, "pe", pe_val)
            if ch > 1 and gidx == 0:
                wait(S, "ve", ve_z_hist[dpar])
            psl = p_proj[:, par * TCH:(par + 1) * TCH]
            bcol = c_bias[:, pj * 4 + j: pj * 4 + j + 1]
            if pj == 0:
                dst = m_dt[:, dof + j * TCH: dof + (j + 1) * TCH]
                inc("se", S.activation(out=dst, in_=psl, func=AF.Exp, bias=bcol, scale=1.0))
                if j == 3:
                    S.drain()
                    for jb in range(4):
                        db = m_dt[:, dof + jb * TCH: dof + (jb + 1) * TCH]
                        S.activation(out=db, in_=db, func=AF.Ln, bias=1.0, scale=1.0)
                    S.drain()
                    for jb in range(4):
                        db = m_dt[:, dof + jb * TCH: dof + (jb + 1) * TCH]
                        inc("se", S.activation(out=db, in_=db, func=AF.Copy, bias=1e-4, scale=1.0))
                    S.drain()
            else:
                dst = (m_bt if pj == 1 else m_ct)[:, dof + j * TCH: dof + (j + 1) * TCH]
                inc("se", S.activation(out=dst, in_=psl, func=AF.Identity, bias=bcol, scale=1.0))
            evac_se.append(cnt["se"])
            if gidx == 11:
                pe_evac_prev = [evac_se[8], evac_se[9], evac_se[10], evac_se[11]]
                fe_state[ch] = (evac_se[7], cnt["se"])

        emit_frontend(0)
        ev0 = []
        mm_pe0 = []
        for g in range(12):
            mm_pe0.append(emit_proj_mm(0, g, ev0))
            emit_proj_evac(0, g, mm_pe0[g], ev0)
        pending = {}
        for ch in range(NCHUNK):
            t0 = ch * TCH
            dpar = ch % 2
            dof = dpar * NDB * TCH
            usp = ch % 2
            se_bt_done, se_proj_done = fe_state[ch]

            wait(V, "se", se_bt_done)
            ds_us[usp].wait(V)
            for j in range(NDB):
                V.tensor_tensor(out=m_bt[:, dof + j * TCH: dof + (j + 1) * TCH],
                                in0=m_dt[:, dof + j * TCH: dof + (j + 1) * TCH],
                                in1=m_bt[:, dof + j * TCH: dof + (j + 1) * TCH], op=ALU.mult)
            V.drain()
            for j in range(NDB):
                V.tensor_tensor(out=m_bt[:, dof + j * TCH: dof + (j + 1) * TCH],
                                in0=m_bt[:, dof + j * TCH: dof + (j + 1) * TCH],
                                in1=m_us[:, (usp * NDB + j) * TCH:(usp * NDB + j + 1) * TCH],
                                op=ALU.mult)
            V.drain()
            if ch > 0:
                wait(V, "pe", pe_tr_done)
            inc("ve", V.engine_nop())
            V.drain()
            ve_beta = cnt["ve"]

            def emit_yacc(pg):
                ppar, poff, pj, png = pg
                buf = m_yv if pj < 2 else m_yg
                jo = (pj if pj < 2 else pj - 2) * TCH
                for nn in range(4):
                    n = png * 4 + nn
                    bc = c_bccol[:, pj * NST + n: pj * NST + n + 1]
                    if png == 0 and nn == 0:
                        V.tensor_scalar(out=buf[:, jo:jo + TCH],
                                        in0=m_xs[:, poff + nn * TCH: poff + (nn + 1) * TCH],
                                        scalar1=bc, scalar2=None, op0=ALU.mult)
                    else:
                        V.scalar_tensor_tensor(out=buf[:, jo:jo + TCH],
                                               in0=m_xs[:, poff + nn * TCH: poff + (nn + 1) * TCH],
                                               scalar=bc,
                                               in1=buf[:, jo:jo + TCH],
                                               op0=ALU.mult, op1=ALU.add)
                    V.drain()

            def emit_carries(cp):
                cpar, coff, cj, cng, cve = cp
                wait(S, "ve", cve)
                for nn in range(4):
                    n = cng * 4 + nn
                    cc = S.copy(c_carry[:, cj * NST + n: cj * NST + n + 1],
                                m_xs[:, coff + nn * TCH + TCH - 1: coff + nn * TCH + TCH])
                    if nn == 3:
                        inc("se", cc)
                S.drain()
                se_carry[(cng, cj)] = cnt["se"]

            prev_yacc = None
            carry_prev = None
            for ng in range(4):
                for j in range(NDB):
                    par = (ng * 4 + j) % 2
                    off = par * 4 * TCH
                    wait(S, "ve", ve_scans_hist[par])
                    if j == 0 and ng == 0:
                        wait(S, "ve", ve_beta)
                    for nn in range(4):
                        n = ng * 4 + nn
                        inc("se", S.activation(out=m_ab[:, off + nn * TCH: off + (nn + 1) * TCH],
                                               in_=m_dt[:, dof + j * TCH: dof + (j + 1) * TCH],
                                               func=AF.Exp, bias=0.0,
                                               scale=c_acol[:, j * NST + n: j * NST + n + 1]))
                    se_d1 = cnt["se"]
                    if carry_prev is not None:
                        emit_carries(carry_prev)
                        carry_prev = None
                    gidx16 = ng * 4 + j
                    if ch + 2 < NCHUNK:
                        emit_convert(uT, uTb_d, gidx16, (ch + 2) * TCH, TCH)
                    emit_wmean_unit(ch * 16 + gidx16, m_wmt, p_wm, ve_consts)
                    if gidx16 == 1 and ch + 1 < NCHUNK:
                        emit_frontend(ch + 1)
                        pending[ch + 1] = ([], [], 0, 0)
                    if ch + 1 in pending and gidx16 >= 2:
                        ev, mp, nm, ne = pending[ch + 1]
                        if nm < 12:
                            mp.append(emit_proj_mm(ch + 1, nm, ev))
                            nm += 1
                        if nm - ne >= 4 or (nm == 12 and ne < 12):
                            emit_proj_evac(ch + 1, ne, mp[ne], ev)
                            ne += 1
                        pending[ch + 1] = (ev, mp, nm, ne)
                    wait(V, "se", se_d1)
                    if ch > 0:
                        wait(V, "se", se_carry.get((ng, j), 0))
                    for nn in range(4):
                        n = ng * 4 + nn
                        inc("ve", V.tensor_tensor_scan(out=m_xs[:, off + nn * TCH: off + (nn + 1) * TCH],
                                                       data0=m_ab[:, off + nn * TCH: off + (nn + 1) * TCH],
                                                       data1=m_bt[:, dof + j * TCH: dof + (j + 1) * TCH],
                                                       initial=c_carry[:, j * NST + n: j * NST + n + 1],
                                                       op0=ALU.mult, op1=ALU.add))
                    if prev_yacc is not None:
                        emit_yacc(prev_yacc)
                        prev_yacc = None
                    V.drain()
                    inc("ve", V.engine_nop())
                    ve_scans_hist[par] = cnt["ve"]
                    ve_scans = cnt["ve"]
                    if ch < NCHUNK - 1:
                        carry_prev = (par, off, j, ng, cnt["ve"])
                    prev_yacc = (par, off, j, ng)

            if prev_yacc is not None:
                emit_yacc(prev_yacc)
                prev_yacc = None
                V.drain()
                inc("ve", V.engine_nop())
            if carry_prev is not None:
                emit_carries(carry_prev)
                carry_prev = None

            if ch + 1 in pending:
                ev, mp, nm, ne = pending.pop(ch + 1)
                while nm < 12 or ne < 12:
                    if nm < 12:
                        mp.append(emit_proj_mm(ch + 1, nm, ev))
                        nm += 1
                    if ne < nm:
                        emit_proj_evac(ch + 1, ne, mp[ne], ev)
                        ne += 1
            for j in range(NDB):
                buf = m_yv if j < 2 else m_yg
                jo = (j if j < 2 else j - 2) * TCH
                V.tensor_tensor(out=buf[:, jo:jo + TCH],
                                in0=buf[:, jo:jo + TCH],
                                in1=m_ct[:, dof + j * TCH: dof + (j + 1) * TCH], op=ALU.mult)
            V.drain()
            for j in range(NDB):
                buf = m_yv if j < 2 else m_yg
                jo = (j if j < 2 else j - 2) * TCH
                V.scalar_tensor_tensor(out=buf[:, jo:jo + TCH],
                                       in0=m_us[:, (usp * NDB + j) * TCH:(usp * NDB + j + 1) * TCH],
                                       scalar=c_d1col[:, j:j + 1],
                                       in1=buf[:, jo:jo + TCH],
                                       op0=ALU.mult, op1=ALU.add)
            inc("ve", V.engine_nop())
            V.drain()
            ve_z_done = cnt["ve"]
            ve_z_hist[dpar] = cnt["ve"]

            for ts_ in range(4):
                zp = ts_ % 2
                wait(T, "ve", ve_z_done)
                if not (ch == 0 and ts_ < 2):
                    ds_zt[zp].wait(T)
                wait(T, "se")
                for j in range(NDB):
                    buf = m_yv if j < 2 else m_yg
                    jo = (j if j < 2 else j - 2) * TCH
                    mmt = T.transpose(p_tr[:, j * 128:(j + 1) * 128],
                                      buf[:, jo + ts_ * 128: jo + (ts_ + 1) * 128],
                                      c_idf[:])
                    if j == 3:
                        inc("pe", mmt)
                wait(S, "pe")
                inc("se", S.copy(m_zT[:, zp * DSL:(zp + 1) * DSL], p_tr[:]))
                wait(G, "se")
                ds_zt[zp].inc(G.dma_start(
                    out=bass.AP(a2a_in, (t0 + ts_ * 128) * DSL, [[DSL, 128], [1, DSL]]),
                    in_=m_zT[:, zp * DSL:(zp + 1) * DSL]))
            pe_tr_done = cnt["pe"]

        wait(V, "pe", wm_state["pe_last"])
        V.tensor_reduce(out=c_mrow[:, 0:1], in_=p_wm[0:1, 0:512], axis=AX.X, op=ALU.add)
        inc("ve", V.tensor_reduce(out=c_mrow[:, 1:2], in_=p_wm[0:1, 512:768], axis=AX.X, op=ALU.add))
        V.drain()
        ve_mrow = cnt["ve"]
        wait(G, "ve", ve_mrow)
        ds_mean.inc(G.dma_start(out=mean_in.ap(), in_=c_mrow[:]))
        ds_mean.wait(G)
        inc("cc", G.collective_compute("AllReduce", ALU.add,
                                       replica_groups=[list(range(8))],
                                       ins=[mean_in.ap()], outs=[mean_out.ap()]))
        wait(G, "cc")
        ds_mean.inc(G.dma_start(out=c_mrow[:], in_=mean_out.ap()))
        ds_mean.wait(V)
        V.tensor_scalar(out=c_mrow[:, 0:1], in0=c_mrow[:, 0:1],
                        scalar1=float(1.0 / (16384 * 2048)), scalar2=1e-5, op0=ALU.mult, op1=ALU.max)
        inc("ve", V.tensor_scalar(out=c_mrow[:, 1:2], in0=c_mrow[:, 1:2],
                                  scalar1=float(1.0 / (8192 * 2048)), scalar2=1e-5, op0=ALU.mult, op1=ALU.max))
        V.drain()
        with nc.psum_tensor("p_bcast", [128, 2], F32) as p_bcast:
            wait(T, "ve")
            inc("pe", T.matmul(p_bcast[:], c_onesr[:], c_mrow[:, 0:2], start=True, stop=True))
            wait(S, "pe")
            inc("se", S.copy(c_sm[:, 6:8], p_bcast[:]))
            S.drain()
        wait(V, "se")
        V.tensor_scalar(out=c_sm[:, 0:1], in0=c_sm[:, 6:7], scalar1=0.5, scalar2=None, op0=ALU.mult)
        V.tensor_scalar(out=c_sm[:, 1:2], in0=c_sm[:, 6:7], scalar1=-0.5, scalar2=None, op0=ALU.mult)
        V.tensor_scalar(out=c_sm[:, 2:3], in0=c_sm[:, 7:8], scalar1=0.5, scalar2=None, op0=ALU.mult)
        inc("ve", V.tensor_scalar(out=c_sm[:, 3:4], in0=c_sm[:, 7:8], scalar1=-0.5, scalar2=None, op0=ALU.mult))
        V.drain()
        ve_sm = cnt["ve"]

        ds_zt[0].wait(G)
        ds_zt[1].wait(G)
        with allow_ici_all_to_all():
            inc("cc", G.collective_compute("AllToAll", ALU.bypass,
                                           replica_groups=[list(range(8))],
                                           ins=[a2a_in.ap()], outs=[a2a_out.ap()]))
        cc_a2a = cnt["cc"]
        if "z" in dbg:
            wait(G, "cc", cc_a2a)
            ds_dbg.inc(G.dma_start(out=dbg["z"], in_=a2a_out.ap()))

    def zrow_ap(fc):
        seq, toff = fc // 4, (fc % 4) * TFC
        return bass.AP(a2a_out, seq * 4 * 512 * DSL + toff * DSL,
                       [[DSL, 128], [512 * DSL, 4], [1, DSL]])

    with nc.sbuf_tensor("f_qhT", [128, 16 * NFC * TFC], BF16) as f_qhT:
        with (
            nc.sbuf_tensor("f_z", [128, 2 * D], F32) as f_z,
            nc.sbuf_tensor("f_s1", [128, 2 * D], F32) as f_s1,
            nc.sbuf_tensor("f_s2", [128, 2 * D], F32) as f_s2,
            nc.sbuf_tensor("f_i32", [128, 2 * D], I32) as f_i32,
            nc.sbuf_tensor("f_qb", [128, 2 * D], BF16) as f_qb,
            nc.sbuf_tensor("f_rw", [128, 16], F32) as f_rw,
            nc.sbuf_tensor("c_nw", [128, D], F32) as c_nw,
            nc.psum_tensor("p_trb", [128, 2048], BF16) as p_trb,
        ):
            wait(Y, "cc", cc_a2a)
            ds_nw.inc(Y.dma_start(out=c_nw[:], in_=nwrep))
            se_x1_hist = [0, 0]
            ve_endtick_hist = [0, 0]
            pe_qb_hist = [0, 0]
            se_tr_hist = [0, 0]
            sch = {}
            ve_qb = {}

            def rwc(fc, c):
                zp = fc % 2
                return f_rw[:, zp * 8 + c: zp * 8 + c + 1]

            def f1_load(fc):
                zp = fc % 2
                if fc < 2:
                    wait(Y, "cc", cc_a2a)
                else:
                    wait(Y, "se", se_x1_hist[zp])
                ds_fz[zp].inc(Y.dma_start(out=f_z[:, zp * D:(zp + 1) * D], in_=zrow_ap(fc)))

            def f1_reduce(fc):
                zp = fc % 2
                ds_fz[zp].wait(V)
                V.reduce_sum(out=rwc(fc, 1), in_=f_z[:, zp * D:(zp + 1) * D], axis=AX.X)
                V.drain()

            def f1_sq1(fc):
                zp = fc % 2
                ds_fz[zp].wait(S)
                if fc >= 2:
                    wait(S, "ve", ve_endtick_hist[zp])
                inc("se", S.activation(out=f_s2[:, zp * D:(zp + 1) * D],
                                       in_=f_z[:, zp * D:(zp + 1) * D],
                                       func=AF.Square, bias=0.0, scale=1.0,
                                       accum_out=rwc(fc, 0)))
                sch[fc, "sq1"] = cnt["se"]

            def f1_st2(fc):
                wait(V, "se", sch[fc, "sq1"])
                V.tensor_scalar(out=c_rows[:, fc:fc + 1], in0=rwc(fc, 1),
                                scalar1=float(1.0 / D), scalar2=None, op0=ALU.mult)
                V.tensor_scalar(out=rwc(fc, 2), in0=rwc(fc, 0),
                                scalar1=float(1.0 / D), scalar2=None, op0=ALU.mult)
                V.drain()
                inc("ve", V.scalar_tensor_tensor(out=rwc(fc, 3), in0=c_rows[:, fc:fc + 1],
                                                 scalar=c_rows[:, fc:fc + 1], in1=rwc(fc, 2),
                                                 op0=ALU.mult, op1=ALU.subtract))
                V.drain()
                sch[fc, "st2"] = cnt["ve"]

            def f1_sqrt1(fc):
                wait(S, "ve", sch[fc, "st2"])
                inc("se", S.activation(out=rwc(fc, 2), in_=rwc(fc, 3), func=AF.Sqrt,
                                       bias=c_sm[:, 4:5], scale=-1.0))
                sch[fc, "sqrt1"] = cnt["se"]

            def f1_st4(fc):
                zp = fc % 2
                wait(V, "se", sch[fc, "sqrt1"])
                if fc >= 2:
                    wait(V, "se", se_x1_hist[zp])
                V.reciprocal(c_rows[:, NFC + fc:NFC + fc + 1], rwc(fc, 2))
                V.drain()
                inc("ve", V.scalar_tensor_tensor(out=rwc(fc, 3), in0=c_rows[:, fc:fc + 1],
                                                 scalar=-1.0, in1=c_rows[:, NFC + fc:NFC + fc + 1],
                                                 op0=ALU.mult, op1=ALU.mult))
                V.drain()
                sch[fc, "st4"] = cnt["ve"]

            def f1_x1(fc):
                zp = fc % 2
                wait(S, "ve", sch[fc, "st4"])
                inc("se", S.activation(out=f_s1[:, zp * D:(zp + 1) * D],
                                       in_=f_z[:, zp * D:(zp + 1) * D],
                                       func=AF.Identity, bias=rwc(fc, 3),
                                       scale=c_rows[:, NFC + fc:NFC + fc + 1]))
                se_x1_hist[zp] = cnt["se"]

            def f1_sq2(fc):
                zp = fc % 2
                inc("se", S.activation(out=f_s2[:, zp * D:(zp + 1) * D],
                                       in_=f_s1[:, zp * D:(zp + 1) * D],
                                       func=AF.Square, bias=0.0, scale=1.0,
                                       accum_out=rwc(fc, 4)))
                sch[fc, "sq2"] = cnt["se"]

            def f1_st6(fc):
                wait(V, "se", sch[fc, "sq2"])
                inc("ve", V.tensor_scalar(out=rwc(fc, 2), in0=rwc(fc, 4),
                                          scalar1=float(1.0 / D), scalar2=None, op0=ALU.mult))
                V.drain()
                sch[fc, "st6"] = cnt["ve"]

            def f1_sqrt2(fc):
                wait(S, "ve", sch[fc, "st6"])
                inc("se", S.activation(out=rwc(fc, 2), in_=rwc(fc, 2), func=AF.Sqrt,
                                       bias=c_sm[:, 4:5], scale=1.0))
                sch[fc, "sqrt2"] = cnt["se"]

            def f1_st8(fc):
                zp = fc % 2
                zD = zp * D
                wait(V, "se", sch[fc, "sqrt2"])
                V.reciprocal(rwc(fc, 5), rwc(fc, 2))
                V.drain()
                if fc == 0:
                    ds_nw.wait(V)
                V.scalar_tensor_tensor(out=f_s2[:, zD:zD + D], in0=f_s1[:, zD:zD + D],
                                       scalar=rwc(fc, 5), in1=c_nw[:],
                                       op0=ALU.mult, op1=ALU.mult)
                V.drain()
                V.reduce_max(out=rwc(fc, 6), in_=f_s2[:, zD:zD + D], axis=AX.X,
                             apply_absolute_value=True)
                V.drain()
                V.tensor_scalar(out=rwc(fc, 6), in0=rwc(fc, 6), scalar1=1e-5, scalar2=None,
                                op0=ALU.max)
                V.drain()
                V.reciprocal(rwc(fc, 7), rwc(fc, 6))
                V.drain()
                V.tensor_scalar(out=rwc(fc, 7), in0=rwc(fc, 7), scalar1=127.0, scalar2=None,
                                op0=ALU.mult)
                V.tensor_scalar(out=rwc(fc, 6), in0=rwc(fc, 6), scalar1=float(1.0 / 127.0),
                                scalar2=None, op0=ALU.mult)
                V.drain()
                V.tensor_scalar(out=f_i32[:, zD:zD + D], in0=f_s2[:, zD:zD + D],
                                scalar1=rwc(fc, 7), scalar2=None, op0=ALU.mult)
                V.drain()
                wait(V, "pe", pe_qb_hist[zp])
                inc("ve", V.tensor_scalar(out=f_qb[:, zD:zD + D], in0=f_i32[:, zD:zD + D],
                                          scalar1=rwc(fc, 6), scalar2=None, op0=ALU.mult))
                V.drain()
                ve_qb[fc] = cnt["ve"]
                ve_endtick_hist[zp] = cnt["ve"]

            def f1_tr(fc):
                zp = fc % 2
                zD = zp * D
                for tg in range(4):
                    tp_ = tg % 2
                    to_ = tp_ * 1024
                    wait(T, "ve", ve_qb[fc])
                    wait(T, "se", se_tr_hist[tp_])
                    for jj in range(4):
                        i = tg * 4 + jj
                        mmt = T.transpose(p_trb[:, to_ + jj * 128: to_ + (jj + 1) * 128],
                                          f_qb[:, zD + i * 128: zD + (i + 1) * 128], c_idb[:])
                        if jj == 3:
                            inc("pe", mmt)
                    wait(S, "pe")
                    for jj in range(4):
                        i = tg * 4 + jj
                        si = S.copy(f_qhT[:, i * NFC * TFC + fc * TFC: i * NFC * TFC + (fc + 1) * TFC],
                                    p_trb[:, to_ + jj * 128: to_ + (jj + 1) * 128])
                        if jj == 3:
                            inc("se", si)
                    S.drain()
                    se_tr_hist[tp_] = cnt["se"]
                pe_qb_hist[zp] = cnt["pe"]

            f1_load(0)
            for t in range(NFC + 2):
                if t + 1 < NFC:
                    f1_load(t + 1)
                if t < NFC:
                    f1_reduce(t)
                if t >= 1 and t - 1 < NFC:
                    f1_sq2(t - 1)
                    f1_st6(t - 1)
                if t < NFC:
                    f1_sq1(t)
                if t >= 1 and t - 1 < NFC:
                    f1_sqrt2(t - 1)
                    f1_st8(t - 1)
                if t < NFC:
                    f1_st2(t)
                    f1_sqrt1(t)
                    f1_st4(t)
                    f1_x1(t)
                if t >= 2:
                    f1_tr(t - 2)
            se_qhT_done = cnt["se"]
            if "qh" in dbg:
                wait(G, "se", se_qhT_done)
                ds_dbg.inc(G.dma_start(out=dbg["qh"], in_=f_qhT[:]))

        with (
            nc.sbuf_tensor("f_wst", [128, 2 * 1024], F32) as f_wst,
            nc.sbuf_tensor("f_sg", [128, 2 * 2048], BF16) as f_sg,
            nc.sbuf_tensor("f_wq", [128, 2 * 16 * 1024], BF16) as f_wq,
            nc.sbuf_tensor("f_gs", [128, 4 * 512], F32) as f_gs,
            nc.sbuf_tensor("f_h2", [128, 4 * 512], F32) as f_h2,
            nc.psum_tensor("p_vg", [128, 8 * 512], F32) as p_vg,
        ):
            wait(Y, "ve", ve_sm)
            wait(Y, "se", se_qhT_done)
            wait(S, "ve", ve_sm)
            sg_hist = [0, 0]
            sg_ve_hist = [0, 0]
            h2_ve_hist = [0, 0, 0, 0]
            pe_pair_hist = [0, 0]
            ve_wq_state = {}

            def emit_build_unit(j2, i):
                wpar = j2 % 2
                woff = wpar * 16 * 1024
                sl = i % 2
                if j2 > 0 or i >= 2:
                    wait(Y, "se", sg_hist[sl])
                ds_wg[sl].inc(Y.dma_start(out=f_wst[:, sl * 1024:(sl + 1) * 1024],
                                          in_=wgvp[i * 128:(i + 1) * 128, j2 * 1024:(j2 + 1) * 1024]))
                ds_wg[sl].wait(S)
                if j2 > 0 or i >= 2:
                    wait(S, "ve", sg_ve_hist[sl])
                S.activation(out=f_sg[:, sl * 2048: sl * 2048 + 1024],
                             in_=f_wst[:, sl * 1024:(sl + 1) * 1024],
                             func=AF.Sign, bias=c_sm[:, 1:2], scale=1.0)
                inc("se", S.activation(out=f_sg[:, sl * 2048 + 1024: (sl + 1) * 2048],
                                       in_=f_wst[:, sl * 1024:(sl + 1) * 1024],
                                       func=AF.Sign, bias=c_sm[:, 0:1], scale=1.0))
                sg_hist[sl] = cnt["se"]
                wait(V, "se", sg_hist[sl])
                if i == 0 and j2 >= 2:
                    wait(V, "pe", pe_pair_hist[wpar])
                inc("ve", V.tensor_tensor(out=f_wq[:, woff + i * 1024: woff + (i + 1) * 1024],
                                          in0=f_sg[:, sl * 2048: sl * 2048 + 1024],
                                          in1=f_sg[:, sl * 2048 + 1024: (sl + 1) * 2048],
                                          op=ALU.add))
                sg_ve_hist[sl] = cnt["ve"]
                if i == 15:
                    ve_wq_state[j2] = cnt["ve"]

            for i in range(16):
                emit_build_unit(0, i)
            for j2 in range(NSL):
                wpar = j2 % 2
                woff = wpar * 16 * 1024
                ve_wq = ve_wq_state[j2]
                for fc in range(NFC):
                    fp = fc % 4
                    wait(T, "ve", ve_wq)
                    wait(T, "se", se_qhT_done)
                    if fc >= 4 or j2 > 0:
                        wait(T, "ve", h2_ve_hist[fp])
                    for i in range(16):
                        T.matmul(p_vg[:, (fp * 2) * 512:(fp * 2 + 1) * 512],
                                 f_qhT[:, i * NFC * TFC + fc * TFC: i * NFC * TFC + (fc + 1) * TFC],
                                 f_wq[:, woff + i * 1024: woff + i * 1024 + 512],
                                 start=(i == 0), stop=(i == 15))
                    for i in range(16):
                        mmg = T.matmul(p_vg[:, (fp * 2 + 1) * 512:(fp * 2 + 2) * 512],
                                       f_qhT[:, i * NFC * TFC + fc * TFC: i * NFC * TFC + (fc + 1) * TFC],
                                       f_wq[:, woff + i * 1024 + 512: woff + (i + 1) * 1024],
                                       start=(i == 0), stop=(i == 15))
                        if i == 15:
                            inc("pe", mmg)
                    if j2 + 1 < NSL:
                        emit_build_unit(j2 + 1, 2 * fc)
                        emit_build_unit(j2 + 1, 2 * fc + 1)
                    wait(S, "pe")
                    inc("se", S.activation(out=f_gs[:, fp * 512:(fp + 1) * 512],
                                           in_=p_vg[:, (fp * 2 + 1) * 512:(fp * 2 + 2) * 512],
                                           func=AF.Silu, bias=0.0, scale=c_sm[:, 0:1]))
                    wait(V, "se")
                    if fc >= 4 or j2 > 0:
                        ds_h2[fp].wait(V)
                    V.scalar_tensor_tensor(out=f_h2[:, fp * 512:(fp + 1) * 512],
                                           in0=p_vg[:, (fp * 2) * 512:(fp * 2 + 1) * 512],
                                           scalar=c_sm[:, 0:1],
                                           in1=f_gs[:, fp * 512:(fp + 1) * 512],
                                           op0=ALU.mult, op1=ALU.mult)
                    V.drain()
                    V.reduce_max(out=c_rmax2[:, fc * NSL + j2: fc * NSL + j2 + 1],
                                 in_=f_h2[:, fp * 512:(fp + 1) * 512],
                                 axis=AX.X, apply_absolute_value=True)
                    inc("ve", V.engine_nop())
                    V.drain()
                    h2_ve_hist[fp] = cnt["ve"]
                    wait(G, "ve", h2_ve_hist[fp])
                    ds_h2[fp].inc(G.dma_start(
                        out=bass.AP(h2sp, fc * TFC * HID + j2 * 512, [[HID, 128], [1, 512]]),
                        in_=f_h2[:, fp * 512:(fp + 1) * 512]))
                pe_pair_hist[wpar] = cnt["pe"]
            ve_f2_done = cnt["ve"]
            if "h2" in dbg:
                for _h in ds_h2:
                    _h.wait(G)
                ds_dbg.inc(G.dma_start(out=dbg["h2"], in_=h2sp.ap()))

    with (
        nc.sbuf_tensor("f_h2r", [128, 2 * HID], F32) as f_h2r,
        nc.sbuf_tensor("f_i2", [128, 2 * HID], I32) as f_i2,
        nc.sbuf_tensor("f_qb2", [128, 2 * HID], BF16) as f_qb2,
        nc.sbuf_tensor("f_qt2", [128, 2 * HID], BF16) as f_qt2,
        nc.sbuf_tensor("f_rw2", [128, 4], F32) as f_rw2,
        nc.psum_tensor("p_trb2", [128, 2048], BF16) as p_trb2,
    ):
        ve_h2q_hist = [0, 0]
        pe_tr2_hist = [0, 0]
        se_tr2_hist = [0, 0]
        q2_n_hist = [0, 0]
        for _h in ds_h2:
            _h.wait(Y)
        for fc in range(NFC):
            par = fc % 2
            ho = par * HID
            if fc >= 2:
                wait(Y, "ve", ve_h2q_hist[par])
            ds_h2r[par].inc(Y.dma_start(out=f_h2r[:, ho:ho + HID],
                                        in_=bass.AP(h2sp, fc * TFC * HID, [[HID, 128], [1, HID]])))
            ds_h2r[par].wait(V)
            if fc == 0:
                wait(V, "ve", ve_f2_done)
            V.reduce_max(out=f_rw2[:, par * 2: par * 2 + 1],
                         in_=c_rmax2[:, fc * NSL:(fc + 1) * NSL], axis=AX.X)
            V.drain()
            V.tensor_scalar(out=f_rw2[:, par * 2: par * 2 + 1], in0=f_rw2[:, par * 2: par * 2 + 1],
                            scalar1=1e-5, scalar2=None, op0=ALU.max)
            V.drain()
            V.reciprocal(f_rw2[:, par * 2 + 1: par * 2 + 2], f_rw2[:, par * 2: par * 2 + 1])
            V.drain()
            V.tensor_scalar(out=f_rw2[:, par * 2 + 1: par * 2 + 2],
                            in0=f_rw2[:, par * 2 + 1: par * 2 + 2],
                            scalar1=127.0, scalar2=None, op0=ALU.mult)
            V.tensor_scalar(out=f_rw2[:, par * 2: par * 2 + 1],
                            in0=f_rw2[:, par * 2: par * 2 + 1],
                            scalar1=float(1.0 / 127.0), scalar2=None, op0=ALU.mult)
            V.drain()
            V.tensor_scalar(out=f_i2[:, ho:ho + HID], in0=f_h2r[:, ho:ho + HID],
                            scalar1=f_rw2[:, par * 2 + 1: par * 2 + 2], scalar2=None, op0=ALU.mult)
            V.drain()
            if fc >= 2:
                wait(V, "pe", pe_tr2_hist[par])
            inc("ve", V.tensor_scalar(out=f_qb2[:, ho:ho + HID], in0=f_i2[:, ho:ho + HID],
                                      scalar1=f_rw2[:, par * 2: par * 2 + 1],
                                      scalar2=None, op0=ALU.mult))
            V.drain()
            ve_qb2 = cnt["ve"]
            ve_h2q_hist[par] = cnt["ve"]
            for tg in range(16):
                tp_ = tg % 2
                to_ = tp_ * 1024
                wait(T, "ve", ve_qb2)
                wait(T, "se", se_tr2_hist[tp_])
                for jj in range(4):
                    k = tg * 4 + jj
                    mmt = T.transpose(p_trb2[:, to_ + jj * 128: to_ + (jj + 1) * 128],
                                      f_qb2[:, ho + k * 128: ho + (k + 1) * 128], c_idb[:])
                    if jj == 3:
                        inc("pe", mmt)
                wait(S, "pe")
                if tg < 2 and fc >= 2:
                    ds_q2.wait(S, q2_n_hist[par])
                inc("se", S.copy(f_qt2[:, ho + tg * 512: ho + (tg + 1) * 512],
                                 p_trb2[:, to_: to_ + 512]))
                se_tr2_hist[tp_] = cnt["se"]
            pe_tr2_hist[par] = cnt["pe"]
            se_qt = cnt["se"]
            wait(G, "se", se_qt)
            ds_q2.inc(G.dma_start(
                out=bass.AP(qh2sp, fc * TFC, [[64 * NFC * TFC, 128], [NFC * TFC, 64], [1, TFC]]),
                in_=f_qt2[:, ho:ho + HID]))
            q2_n_hist[par] = ds_q2.n

    with (
        nc.sbuf_tensor("f_ob", [128, NFC * D], F32) as f_ob,
        nc.sbuf_tensor("f_wst2", [128, 4 * 512], F32) as f_wst2,
        nc.sbuf_tensor("f_sg2", [128, 4 * 1024], BF16) as f_sg2,
        nc.sbuf_tensor("f_wq2", [128, 4 * 512], BF16) as f_wq2,
        nc.sbuf_tensor("f_qt", [128, 4 * 1024], BF16) as f_qt,
        nc.psum_tensor("p_o8", [128, 8 * 512], F32) as p_o8,
    ):
        pe_k_hist = []
        ve_wq2 = {}
        sg2_hist = [0, 0, 0, 0]
        se_evac_dsl = 0
        ds_q2.wait(Y)

        def emit_build2(kk):
            dsl_, k_ = kk // 64, kk % 64
            sl = kk % 4
            if kk >= 4:
                wait(G, "pe", pe_k_hist[kk - 4])
            ds_w2[sl].inc(G.dma_start(out=f_wst2[:, sl * 512:(sl + 1) * 512],
                                      in_=woutT[k_ * 128:(k_ + 1) * 128, dsl_ * 512:(dsl_ + 1) * 512]))
            ds_w2[sl].wait(S)
            if kk >= 4:
                wait(S, "ve", sg2_hist[sl])
            S.activation(out=f_sg2[:, sl * 1024: sl * 1024 + 512],
                         in_=f_wst2[:, sl * 512:(sl + 1) * 512],
                         func=AF.Sign, bias=c_sm[:, 3:4], scale=1.0)
            inc("se", S.activation(out=f_sg2[:, sl * 1024 + 512:(sl + 1) * 1024],
                                   in_=f_wst2[:, sl * 512:(sl + 1) * 512],
                                   func=AF.Sign, bias=c_sm[:, 2:3], scale=1.0))
            wait(V, "se", cnt["se"])
            if kk >= 4:
                wait(V, "pe", pe_k_hist[kk - 4])
            inc("ve", V.tensor_tensor(out=f_wq2[:, sl * 512:(sl + 1) * 512],
                                      in0=f_sg2[:, sl * 1024: sl * 1024 + 512],
                                      in1=f_sg2[:, sl * 1024 + 512:(sl + 1) * 1024],
                                      op=ALU.add))
            sg2_hist[sl] = cnt["ve"]
            ve_wq2[kk] = cnt["ve"]

        def emit_qt(kk):
            dsl_, k_ = kk // 64, kk % 64
            sl = kk % 4
            if kk >= 4:
                wait(Y, "pe", pe_k_hist[kk - 4])
            ds_qt[sl].inc(Y.dma_start(out=f_qt[:, sl * 1024:(sl + 1) * 1024],
                                      in_=qh2sp.ap()[:, k_ * NFC * TFC:(k_ + 1) * NFC * TFC]))

        for kk in range(3):
            emit_build2(kk)
            emit_qt(kk)
        for dsl in range(4):
            for k in range(64):
                sl = (dsl * 64 + k) % 4
                kk = dsl * 64 + k
                wait(T, "ve", ve_wq2[kk])
                ds_qt[sl].wait(T)
                if k == 0 and dsl > 0:
                    wait(T, "se", se_evac_dsl)
                for fc in range(NFC):
                    mm = T.matmul(p_o8[:, fc * 512:(fc + 1) * 512],
                                  f_qt[:, sl * 1024 + fc * TFC: sl * 1024 + (fc + 1) * TFC],
                                  f_wq2[:, sl * 512:(sl + 1) * 512],
                                  start=(k == 0), stop=(k == 63))
                    if fc == 7:
                        inc("pe", mm)
                pe_k_hist.append(cnt["pe"])
                if kk + 3 < 256:
                    emit_build2(kk + 3)
                    emit_qt(kk + 3)
            wait(S, "pe")
            for fc in range(NFC):
                si = S.activation(out=f_ob[:, fc * D + dsl * 512: fc * D + (dsl + 1) * 512],
                                  in_=p_o8[:, fc * 512:(fc + 1) * 512],
                                  func=AF.Copy, bias=0.0, scale=c_sm[:, 2:3])
                if fc == 7:
                    inc("se", si)
            S.drain()
            se_evac_dsl = cnt["se"]
        se_ob_done = cnt["se"]
        if "ob" in dbg:
            wait(G, "se", se_ob_done)
            ds_dbg.inc(G.dma_start(out=dbg["ob"], in_=f_ob[:]))

        with (
            nc.sbuf_tensor("g_z", [128, D], F32) as g_z,
            nc.sbuf_tensor("g_x2", [128, 2 * D], F32) as g_x2,
            nc.sbuf_tensor("g_sc", [128, D], F32) as g_sc,
            nc.sbuf_tensor("g_rw", [128, 8], F32) as g_rw,
        ):
            ve_g_hist = 0
            wait(Y, "se", se_ob_done)
            for fc in range(NFC):
                xp = fc % 2
                if fc > 0:
                    wait(Y, "ve", ve_g_hist)
                ds_gz.inc(Y.dma_start(out=g_z[:], in_=zrow_ap(fc)))
                ds_gz.wait(V)
                wait(V, "se", se_ob_done)
                if fc >= 1:
                    ds_out[(fc - 1) % 2].wait(V)
                V.tensor_scalar(out=g_sc[:], in0=g_z[:],
                                scalar1=c_rows[:, fc:fc + 1], scalar2=c_rows[:, NFC + fc:NFC + fc + 1],
                                op0=ALU.subtract, op1=ALU.mult)
                V.drain()
                inc("ve", V.scalar_tensor_tensor(out=g_x2[:, xp * D:(xp + 1) * D],
                                                 in0=g_sc[:], scalar=2.0,
                                                 in1=f_ob[:, fc * D:(fc + 1) * D],
                                                 op0=ALU.mult, op1=ALU.add))
                V.drain()
                ve_x2 = cnt["ve"]
                ve_g_hist = ve_x2
                x2sl = g_x2[:, xp * D:(xp + 1) * D]
                wait(S, "ve", ve_x2)
                inc("se", S.activation(out=g_sc[:], in_=x2sl, func=AF.Square, bias=0.0, scale=1.0,
                                       accum_out=g_rw[:, 0:1]))
                S.drain()
                V.reduce_sum(out=g_rw[:, 1:2], in_=x2sl, axis=AX.X)
                V.drain()
                wait(V, "se")
                V.tensor_scalar(out=g_rw[:, 1:2], in0=g_rw[:, 1:2], scalar1=float(1.0 / D),
                                scalar2=None, op0=ALU.mult)
                V.tensor_scalar(out=g_rw[:, 2:3], in0=g_rw[:, 0:1], scalar1=float(1.0 / D),
                                scalar2=None, op0=ALU.mult)
                V.drain()
                inc("ve", V.scalar_tensor_tensor(out=g_rw[:, 3:4], in0=g_rw[:, 1:2],
                                                 scalar=g_rw[:, 1:2], in1=g_rw[:, 2:3],
                                                 op0=ALU.mult, op1=ALU.subtract))
                V.drain()
                wait(S, "ve")
                inc("se", S.activation(out=g_rw[:, 4:5], in_=g_rw[:, 3:4], func=AF.Sqrt,
                                       bias=c_sm[:, 4:5], scale=-1.0))
                S.drain()
                wait(V, "se")
                V.reciprocal(g_rw[:, 5:6], g_rw[:, 4:5])
                V.drain()
                inc("ve", V.tensor_scalar(out=g_sc[:], in0=x2sl,
                                          scalar1=g_rw[:, 1:2], scalar2=g_rw[:, 5:6],
                                          op0=ALU.subtract, op1=ALU.mult))
                V.drain()
                ve_out = cnt["ve"]
                wait(G, "ve", ve_out)
                ds_out[fc % 2].inc(G.dma_start(out=out[fc * TFC:(fc + 1) * TFC, :], in_=g_sc[:]))
            ds_out[0].wait(G)
            ds_out[1].wait(G)
            ds_dbg.wait(G)

    ctx.close()
    return nc


def prep_core_inputs(ins, cid):
    g, q = cid // 4, cid % 4
    f32 = np.float32
    u = np.asarray(ins["L0_attn_u"])[g]
    sl = slice(q * DSL, (q + 1) * DSL)
    uT_ = np.ascontiguousarray(u.T)
    uS_ = np.ascontiguousarray(uT_[sl])
    Wdt, Wb, Wc = (np.asarray(ins[k]) for k in ("L0_attn_Wdt", "L0_attn_Wb", "L0_attn_Wc"))
    w3_ = np.ascontiguousarray(np.concatenate([Wdt[:, sl], Wb[:, sl], Wc[:, sl]], axis=1))
    def colmat(v):
        return np.ascontiguousarray(np.asarray(v)[sl].reshape(NDB, 128).T.astype(f32))
    biases_ = np.ascontiguousarray(np.concatenate(
        [colmat(ins["L0_attn_bdt"]), colmat(ins["L0_attn_bb"]), colmat(ins["L0_attn_bc"])], axis=1))
    def colmat16(m):
        a = np.asarray(m)[sl].reshape(NDB, 128, NST)
        return np.ascontiguousarray(a.transpose(1, 0, 2).reshape(128, NDB * NST).astype(f32))
    araw_ = colmat16(ins["L0_attn_a_raw"])
    bb_ = colmat16(ins["L0_attn_B_base"])
    cb_ = colmat16(ins["L0_attn_C_base"])
    dvec_ = colmat(ins["L0_attn_D"])
    nwrep_ = np.ascontiguousarray(np.broadcast_to(
        np.asarray(ins["L0_ffn_norm_w"])[None, :], (128, D)).astype(f32))
    w_gv = np.asarray(ins["L0_ffn_w_gv"])
    wgvT = w_gv.T
    cols = []
    for j in range(NSL):
        cols.append(wgvT[:, j * 512:(j + 1) * 512])
        cols.append(wgvT[:, HID + j * 512: HID + (j + 1) * 512])
    wgvp_ = np.ascontiguousarray(np.concatenate(cols, axis=1))
    woutT_ = np.ascontiguousarray(np.asarray(ins["L0_ffn_w_out"]).T)
    wmg_ = np.ascontiguousarray(wgvT[:, cid * 2048:(cid + 1) * 2048])
    wmo_ = np.ascontiguousarray(woutT_[:, cid * 256:(cid + 1) * 256])
    return {
        "uT": uT_.astype(f32), "uS": uS_.astype(f32), "w3": w3_.astype(f32),
        "biases": biases_, "araw": araw_, "bbase": bb_, "cbase": cb_,
        "dvec": dvec_, "nwrep": nwrep_,
        "idb": np.eye(128, dtype=ml_dtypes.bfloat16), "idf": np.eye(128, dtype=f32),
        "wgvp": wgvp_.astype(f32), "woutT": woutT_.astype(f32),
        "wmg": wmg_.astype(f32), "wmo": wmo_.astype(f32),
    }


def kernel(**inputs):
    nc = build_nc()
    in_maps = [prep_core_inputs(inputs, cid) for cid in range(8)]
    res = run_bass_kernel_spmd(nc, in_maps, core_ids=list(range(8)))
    outp = np.empty((2, L, D), dtype=np.float32)
    for cid in range(8):
        o = res.results[cid]["out"]
        outp[0, cid * 512:(cid + 1) * 512, :] = o[0:512]
        outp[1, cid * 512:(cid + 1) * 512, :] = o[512:1024]
    return outp
